# version 01
# baseline (speedup 1.0000x reference)
import contextlib
import numpy as np
import ml_dtypes
import concourse.bass as bass
import concourse.mybir as mybir
from concourse.bass_utils import run_bass_kernel_spmd

F32 = mybir.dt.float32
BF16 = mybir.dt.bfloat16
I32 = mybir.dt.int32
AF = mybir.ActivationFunctionType
ALU = mybir.AluOpType
AX = mybir.AxisListType

T = 8192
D = 2048
HID = 8192
NCORE = 8
TPC = 8
NT = T // 128
NDK = D // 128
NKB = T // 512
EPS = 1e-5

ENGINES = ("sync", "gpsimd", "vector", "scalar", "tensor")
DEBUG = False


class Builder:
    def __init__(self, nc, es):
        self.nc = nc
        self.es = es
        self.sems = {}
        self.cnt = {}
        self.seen = {}
        self.plan = {e: [] for e in ENGINES}

    def sem(self, name):
        if name not in self.sems:
            self.sems[name] = self.es.enter_context(self.nc.semaphore(name))
            self.cnt[name] = 0
        return self.sems[name]

    def val(self, name):
        self.sem(name)
        return self.cnt[name]

    def add(self, eng, fn, deps=(), inc=None, k=1):
        deps = [(s, l) for (s, l) in deps if l > 0]
        for s, _ in deps:
            self.sem(s)
        r = None
        if inc is not None:
            self.sem(inc)
            self.cnt[inc] += k
            r = self.cnt[inc]
        self.plan[eng].append((fn, deps, inc, k))
        return r

    def barrier(self, name):
        for e in ENGINES:
            self.add(e, lambda h: h.nop(), inc=name)
        lvl = self.val(name)
        for e in ENGINES:
            self.add(e, None, deps=[(name, lvl)])

    def emit(self, block):
        plan = self.plan
        self.plan = {e: [] for e in ENGINES}

        def runner(eng_name):
            def run(handle):
                for fn, deps, inc, k in plan[eng_name]:
                    for s, lvl in deps:
                        if self.seen.get((eng_name, s), 0) < lvl:
                            handle.wait_ge(self.sems[s], lvl)
                            self.seen[(eng_name, s)] = lvl
                    if fn is not None:
                        inst = fn(handle)
                        if inc is not None:
                            inst.then_inc(self.sems[inc], k)
            return run

        for name, opener in (("sync", block.sync), ("gpsimd", block.gpsimd),
                             ("vector", block.vector), ("scalar", block.scalar),
                             ("tensor", block.tensor)):
            if plan[name]:
                opener(runner(name))


def build_nc(debug=DEBUG):
    nc = bass.Bass("TRN2", target_bir_lowering=False)
    es = contextlib.ExitStack()
    B = Builder(nc, es)

    def din(name, shape, dt=F32):
        return nc.dram_tensor(name, shape, dt, kind="ExternalInput").ap()

    def dsc(name, shape, dt):
        return nc.dram_tensor(name, shape, dt, kind="Internal").ap()

    x_full = din("x_full", [T, D])
    x_own_i = din("x_own", [128 * TPC, D])
    qs_in = din("qs_shift", [128, TPC * NKB])
    kpos_in = din("kpos_in", [128, 512])
    Wq_i = din("Wq", [D, D]); Wk_i = din("Wk", [D, D]); Wv_i = din("Wv", [D, D])
    bq_i = din("bq_t", [128, NDK]); bk_i = din("bk_t", [128, NDK])
    bv_i = din("bv_r", [1, D])
    gw_i = din("gate_w", [HID, D]); vw_i = din("val_w", [HID, D])
    ow_i = din("out_w", [D, HID])
    gb_i = din("gate_b_r", [1, HID]); vb_i = din("val_b_r", [1, HID])
    outb_i = din("out_b_rep", [128, D])

    y_out = nc.dram_tensor("y_out", [128 * TPC, D], F32, kind="ExternalOutput").ap()

    H_d = dsc("H_d", [T, D], BF16)
    K_d = dsc("K_d", [D, T], BF16)
    V_d = dsc("V_d", [T, D], BF16)
    Q_d = dsc("Q_d", [NDK, 128, 128 * TPC], BF16)
    X1_d = dsc("X1_d", [128 * TPC, D], F32)
    XQ_d = dsc("XQ_d", [128 * TPC, D], BF16)
    GWC = dsc("GWC", [HID, D], BF16)
    VWC = dsc("VWC", [HID, D], BF16)
    OWC = dsc("OWC", [D, HID], BF16)
    GV_d = dsc("GV_d", [128 * TPC, HID], F32)
    GVQ_d = dsc("GVQ_d", [128 * TPC, HID], BF16)
    Y3_d = dsc("Y3_d", [128 * TPC, D], F32)
    SB_d = dsc("SB_d", [2, 8, 128], F32)

    dbg = {}
    if debug:
        for nm, shape in [("dbg_h", [T, D]), ("dbg_k", [D, T]), ("dbg_v", [T, D]),
                          ("dbg_x1", [128 * TPC, D]), ("dbg_gam", [1, 4]),
                          ("dbg_gv", [128 * TPC, HID]), ("dbg_xq", [128 * TPC, D]),
                          ("dbg_gwq", [HID, D]), ("dbg_gvq", [128 * TPC, HID]),
                          ("dbg_y3", [128 * TPC, D])]:
            dbg[nm] = nc.dram_tensor(nm, shape, F32, kind="ExternalOutput").ap()

    with es:
        block = es.enter_context(nc.Block())

        def sb(name, shape, dt, stack=None):
            return (stack or es).enter_context(nc.sbuf_tensor(name, shape, dt))

        def pt(name, shape, dt, stack=None):
            return (stack or es).enter_context(nc.psum_tensor(name, shape, dt))

        sb_bq = sb("sb_bq", [128, NDK], F32)
        sb_bk = sb("sb_bk", [128, NDK], F32)
        ones_r = sb("ones_r", [1, 128], BF16)
        ones_cf = sb("ones_cf", [128, 1], F32)
        ones_rf = sb("ones_rf", [1, 128], F32)
        eps_c = sb("eps_c", [128, 1], F32)
        ident = sb("ident", [128, 128], BF16)
        ident32 = sb("ident32", [128, 128], F32)
        iota_i = sb("iota_i", [128, 128], I32)
        gamx = sb("gamx", [128, TPC], F32)
        sg_all = sb("sg_all", [128, TPC], F32)
        sv_all = sb("sv_all", [128, TPC], F32)
        so_all = sb("so_all", [128, TPC], F32)
        sinvg_c = sb("sinvg_c", [128, TPC], F32)
        sinvv_c = sb("sinvv_c", [128, TPC], F32)
        gvm_all = sb("gvm_all", [128, TPC * 16], F32)
        gamw = sb("gamw", [128, 3], F32)
        thrp = sb("thrp", [128, 3], F32)
        thrn = sb("thrn", [128, 3], F32)
        gamw_s = sb("gamw_s", [1, 4], F32)
        stats = sb("stats", [128, 16], F32)

        B.add("sync", lambda s: s.dma_start(out=sb_bq[:], in_=bq_i), inc="ld0", k=16)
        B.add("sync", lambda s: s.dma_start(out=sb_bk[:], in_=bk_i), inc="ld0", k=16)
        B.add("gpsimd", lambda g: g.memset(ones_r[:], 1.0), inc="c_g")
        B.add("gpsimd", lambda g: g.memset(ones_cf[:], 1.0), inc="c_g")
        B.add("gpsimd", lambda g: g.memset(ones_rf[:], 1.0), inc="c_g")
        B.add("gpsimd", lambda g: g.memset(eps_c[:], EPS), inc="c_g")
        B.add("gpsimd", lambda g: g.iota(iota_i[:], [[1, 128]], channel_multiplier=-1),
              inc="c_g")
        cg = B.val("c_g")
        B.add("vector", lambda v: v.tensor_single_scalar(
            out=ident[:], in_=iota_i[:], scalar=0, op=ALU.is_equal),
            deps=[("c_g", cg)], inc="c_v")
        B.add("vector", lambda v: v.tensor_copy(out=ident32[:], in_=ident[:]),
              deps=[("c_v", B.val("c_v"))], inc="c_v")
        B.emit(block)

        def plan_ln(xt, h_out, tv, ts, xc, scr, extra_dep=()):
            m = stats[:, 0:1]; ssq = stats[:, 1:2]; var = stats[:, 2:3]
            sq = stats[:, 3:4]; rstd = stats[:, 4:5]
            B.add("vector", lambda v: v.reduce_sum(out=m, in_=xt, axis=AX.X),
                  deps=list(extra_dep) + [(tv, B.val(tv)), (ts, B.val(ts))], inc=tv)
            B.add("vector", lambda v: v.tensor_scalar_mul(out=m, in0=m, scalar1=1.0 / D),
                  deps=[(tv, B.val(tv))], inc=tv)
            B.add("vector", lambda v: v.tensor_scalar(
                out=xc, in0=xt, scalar1=m, scalar2=None, op0=ALU.subtract),
                deps=[(tv, B.val(tv))], inc=tv)
            B.add("scalar", lambda s: s.activation(out=scr, in_=xc, func=AF.Square,
                                                   accum_out=ssq),
                  deps=[(tv, B.val(tv))], inc=ts)
            B.add("vector", lambda v: v.tensor_scalar_mul(out=var, in0=ssq,
                                                          scalar1=1.0 / D),
                  deps=[(ts, B.val(ts))], inc=tv)
            B.add("scalar", lambda s: s.activation(out=sq, in_=var, func=AF.Sqrt,
                                                   bias=eps_c[:]),
                  deps=[(tv, B.val(tv))], inc=ts)
            B.add("vector", lambda v: v.reciprocal(out=rstd, in_=sq),
                  deps=[(ts, B.val(ts))], inc=tv)
            B.add("vector", lambda v: v.tensor_scalar(
                out=h_out, in0=xc, scalar1=rstd, scalar2=None, op0=ALU.mult),
                deps=[(tv, B.val(tv))], inc=tv)

        def plan_round(src, out_bf, ti32, fa, fb, tv):
            B.add("vector", lambda v: v.tensor_copy(out=ti32, in_=src),
                  deps=[(tv, B.val(tv))], inc=tv)
            B.add("vector", lambda v: v.tensor_sub(out=fa, in0=src, in1=ti32),
                  deps=[(tv, B.val(tv))], inc=tv)
            B.add("vector", lambda v: v.tensor_single_scalar(
                out=fb, in_=fa, scalar=-0.5, op=ALU.is_lt),
                deps=[(tv, B.val(tv))], inc=tv)
            B.add("vector", lambda v: v.scalar_tensor_tensor(
                out=fb, in0=fa, scalar=0.5, in1=fb, op0=ALU.is_gt, op1=ALU.subtract),
                deps=[(tv, B.val(tv))], inc=tv)
            B.add("vector", lambda v: v.tensor_add(out=out_bf, in0=fb, in1=ti32),
                  deps=[(tv, B.val(tv))], inc=tv)

        with contextlib.ExitStack() as ps:
            p1x = sb("p1x", [128, 2 * D], F32, ps)
            p1xc = sb("p1xc", [128, D], F32, ps)
            p1scr = sb("p1scr", [128, D], F32, ps)
            p1h = sb("p1h", [128, 2 * D], BF16, ps)
            hfm = sb("hfm", [128, NDK, 2048], BF16, ps)
            wv_sb = sb("wv_sb", [128, NDK, D], BF16, ps)
            wcol = sb("wcol", [128, 2, NDK, 128], BF16, ps)
            kst = sb("kst", [128, 2, 2048], BF16, ps)
            vst = sb("vst", [128, 2, 2048], BF16, ps)
            kvp = [pt("kvp0", [128, 2048], F32, ps), pt("kvp1", [128, 2048], F32, ps)]
            sb_bv = sb("sb_bv", [1, D], BF16, ps)

            B.add("gpsimd", lambda g: g.dma_start(out=sb_bv[:], in_=bv_i),
                  inc="ld0", k=16)
            for d_i in range(NDK):
                B.add("gpsimd", lambda g, d_i=d_i: g.dma_start(
                    out=wv_sb[:, :, d_i * 128:(d_i + 1) * 128],
                    in_=Wv_i.rearrange("(a p) n -> p a n", p=128)[
                        :, :, d_i * 128:(d_i + 1) * 128]),
                    inc="p1wv", k=16)
            wv_lvl = B.val("p1wv")

            kst_store = [0, 0]
            vst_store = [0, 0]
            wcol_use = [0, 0]
            pev_slot = [0, 0]
            for g in range(4):
                for ti in range(16):
                    t = g * 16 + ti
                    slx = t % 2
                    B.add("sync", lambda s, t=t, slx=slx: s.dma_start(
                        out=p1x[:, slx * D:(slx + 1) * D],
                        in_=x_full[t * 128:(t + 1) * 128, :]),
                        deps=[("p1hs", 16 * (t - 1))], inc="p1x", k=16)
                    plan_ln(p1x[:, slx * D:(slx + 1) * D],
                            p1h[:, slx * D:(slx + 1) * D], "p1v", "p1s",
                            p1xc[:], p1scr[:],
                            extra_dep=[("p1x", 16 * (t + 1)), ("p1hs", 16 * (t - 1))])
                    B.add("gpsimd", lambda gg, t=t, slx=slx: gg.dma_start(
                        out=H_d[t * 128:(t + 1) * 128, :],
                        in_=p1h[:, slx * D:(slx + 1) * D]),
                        deps=[("p1v", B.val("p1v"))], inc="p1hs", k=16)

                prev_mm = B.val("p1mm")
                hs_lvl = 16 * 16 * (g + 1)
                for ck in range(NDK):
                    B.add("sync", lambda s, g=g, ck=ck: s.dma_start(
                        out=hfm[:, ck, :],
                        in_=H_d[g * 2048:(g + 1) * 2048, ck * 128:(ck + 1) * 128],
                        transpose=True),
                        deps=[("p1hs", hs_lvl), ("p1mm", prev_mm)],
                        inc="p1ht", k=16)
                ht_lvl = B.val("p1ht")

                for d_i in range(NDK):
                    sl = d_i % 2
                    B.add("gpsimd", lambda gg, d_i=d_i, sl=sl: gg.dma_start(
                        out=wcol[:, sl, :, :],
                        in_=Wk_i.rearrange("(a p) n -> p a n", p=128)[
                            :, :, d_i * 128:(d_i + 1) * 128]),
                        deps=[("p1mm", wcol_use[sl])], inc="p1w", k=16)
                    wl = B.val("p1w")
                    for ck in range(NDK):
                        for tj in range(4):
                            B.add("tensor", lambda tt, sl=sl, ck=ck, tj=tj: tt.matmul(
                                kvp[sl][:, tj * 512:(tj + 1) * 512],
                                wcol[:, sl, ck, :],
                                hfm[:, ck, tj * 512:(tj + 1) * 512],
                                start=(ck == 0), stop=(ck == NDK - 1)),
                                deps=[("p1w", wl), ("p1ht", ht_lvl),
                                      ("p1ev", pev_slot[sl])],
                                inc="p1mm" if (ck == NDK - 1 and tj == 3) else None)
                    wcol_use[sl] = B.val("p1mm")
                    mm_lvl = B.val("p1mm")
                    B.add("vector", lambda v, sl=sl, d_i=d_i: v.tensor_scalar(
                        out=kst[:, sl, :], in0=kvp[sl][:],
                        scalar1=sb_bk[:, d_i:d_i + 1], scalar2=None, op0=ALU.add),
                        deps=[("p1mm", mm_lvl), ("p1ks", kst_store[sl]),
                              ("p1v", B.val("p1v")), ("p1ev", B.val("p1ev"))],
                        inc="p1ev")
                    pev_slot[sl] = B.val("p1ev")
                    B.add("gpsimd", lambda gg, g=g, d_i=d_i, sl=sl: gg.dma_start(
                        out=K_d[d_i * 128:(d_i + 1) * 128, g * 2048:(g + 1) * 2048],
                        in_=kst[:, sl, :]),
                        deps=[("p1ev", B.val("p1ev"))], inc="p1ks", k=16)
                    kst_store[sl] = B.val("p1ks")

                for t_i in range(16):
                    sl = t_i % 2
                    for ck in range(NDK):
                        for dj in range(4):
                            B.add("tensor", lambda tt, sl=sl, ck=ck, dj=dj, t_i=t_i:
                                  tt.matmul(
                                      kvp[sl][:, dj * 512:(dj + 1) * 512],
                                      hfm[:, ck, t_i * 128:(t_i + 1) * 128],
                                      wv_sb[:, ck, dj * 512:(dj + 1) * 512],
                                      start=(ck == 0), stop=False),
                                  deps=[("p1wv", wv_lvl), ("p1ht", ht_lvl),
                                        ("p1ev", pev_slot[sl])])
                    for dj in range(4):
                        B.add("tensor", lambda tt, sl=sl, dj=dj: tt.matmul(
                            kvp[sl][:, dj * 512:(dj + 1) * 512], ones_r[0:1, :],
                            sb_bv[0:1, dj * 512:(dj + 1) * 512],
                            start=False, stop=True),
                            deps=[("ld0", B.val("ld0")), ("c_g", 1)],
                            inc="p1mm" if dj == 3 else None)
                    mm_lvl = B.val("p1mm")
                    B.add("vector", lambda v, sl=sl: v.tensor_copy(
                        out=vst[:, sl, :], in_=kvp[sl][:]),
                        deps=[("p1mm", mm_lvl), ("p1vs", vst_store[sl]),
                              ("p1v", B.val("p1v")), ("p1ev", B.val("p1ev"))],
                        inc="p1ev")
                    pev_slot[sl] = B.val("p1ev")
                    B.add("gpsimd", lambda gg, g=g, t_i=t_i, sl=sl: gg.dma_start(
                        out=V_d[g * 2048 + t_i * 128: g * 2048 + (t_i + 1) * 128, :],
                        in_=vst[:, sl, :]),
                        deps=[("p1ev", B.val("p1ev"))], inc="p1vs", k=16)
                    vst_store[sl] = B.val("p1vs")
            B.emit(block)
        B.barrier("b1")
        B.emit(block)

        with contextlib.ExitStack() as ps:
            qx = sb("qx", [128, 2 * D], F32, ps)
            qxc = sb("qxc", [128, D], F32, ps)
            qscr = sb("qscr", [128, D], F32, ps)
            qh = sb("qh", [128, 2 * D], BF16, ps)
            hqfm = sb("hqfm", [128, NDK, 1024], BF16, ps)
            wqcol = sb("wqcol", [128, 2, NDK, 128], BF16, ps)
            qst = sb("qst", [128, 2, 1024], BF16, ps)
            qps = [pt("qps0", [128, 1024], F32, ps), pt("qps1", [128, 1024], F32, ps)]

            for t in range(TPC):
                slx = t % 2
                B.add("sync", lambda s, t=t, slx=slx: s.dma_start(
                    out=qx[:, slx * D:(slx + 1) * D],
                    in_=x_own_i[t * 128:(t + 1) * 128, :]),
                    deps=[("qhs", 16 * (t - 1))], inc="qx", k=16)
                plan_ln(qx[:, slx * D:(slx + 1) * D], qh[:, slx * D:(slx + 1) * D],
                        "qv", "qss", qxc[:], qscr[:],
                        extra_dep=[("qx", 16 * (t + 1)), ("qhs", 16 * (t - 1))])
                B.add("gpsimd", lambda gg, t=t, slx=slx: gg.dma_start(
                    out=XQ_d[t * 128:(t + 1) * 128, :],
                    in_=qh[:, slx * D:(slx + 1) * D]),
                    deps=[("qv", B.val("qv"))], inc="qhs", k=16)
            hs_lvl = B.val("qhs")
            for ck in range(NDK):
                B.add("sync", lambda s, ck=ck: s.dma_start(
                    out=hqfm[:, ck, :],
                    in_=XQ_d[0:1024, ck * 128:(ck + 1) * 128],
                    transpose=True),
                    deps=[("qhs", hs_lvl)], inc="qht", k=16)
            ht_lvl = B.val("qht")

            qst_store = [0, 0]
            wq_use = [0, 0]
            qev_slot = [0, 0]
            for d_i in range(NDK):
                sl = d_i % 2
                B.add("gpsimd", lambda gg, d_i=d_i, sl=sl: gg.dma_start(
                    out=wqcol[:, sl, :, :],
                    in_=Wq_i.rearrange("(a p) n -> p a n", p=128)[
                        :, :, d_i * 128:(d_i + 1) * 128]),
                    deps=[("qmm", wq_use[sl])], inc="qw", k=16)
                wl = B.val("qw")
                for ck in range(NDK):
                    for su in range(2):
                        B.add("tensor", lambda tt, sl=sl, ck=ck, su=su: tt.matmul(
                            qps[sl][:, su * 512:(su + 1) * 512],
                            wqcol[:, sl, ck, :],
                            hqfm[:, ck, su * 512:(su + 1) * 512],
                            start=(ck == 0), stop=(ck == NDK - 1)),
                            deps=[("qw", wl), ("qht", ht_lvl),
                                  ("qev", qev_slot[sl])],
                            inc="qmm" if (ck == NDK - 1 and su == 1) else None)
                wq_use[sl] = B.val("qmm")
                mm_lvl = B.val("qmm")
                B.add("vector", lambda v, sl=sl, d_i=d_i: v.tensor_scalar(
                    out=qst[:, sl, :], in0=qps[sl][:],
                    scalar1=sb_bq[:, d_i:d_i + 1], scalar2=None, op0=ALU.add),
                    deps=[("qmm", mm_lvl), ("qs2", qst_store[sl]),
                          ("qv", B.val("qv")), ("qev", B.val("qev"))],
                    inc="qev")
                qev_slot[sl] = B.val("qev")
                B.add("gpsimd", lambda gg, d_i=d_i, sl=sl: gg.dma_start(
                    out=Q_d[d_i, :, :], in_=qst[:, sl, :]),
                    deps=[("qev", B.val("qev"))], inc="qs2", k=16)
                qst_store[sl] = B.val("qs2")
            B.emit(block)
        B.barrier("b2")
        B.emit(block)

        with contextlib.ExitStack() as ps:
            kring = sb("kring", [128, 4, NDK, 512], BF16, ps)
            vring = sb("vring", [128, 2, 4, 2048], BF16, ps)
            stash = sb("stash", [128, NKB * 512], F32, ps)
            Pring = sb("Pring", [128, 2 * 1024], BF16, ps)
            PTsb = sb("PTsb", [128, 2 * 1024], BF16, ps)
            mk = sb("mk", [128, 2, 512], F32, ps)
            lparts = sb("lparts", [128, TPC], F32, ps)
            qtile = sb("qtile", [128, 2, NDK, 128], BF16, ps)
            xot = sb("xot", [128, D], F32, ps)
            x1t = sb("x1t", [128, D], F32, ps)
            psS = [pt("psSa", [128, 512], F32, ps), pt("psSb", [128, 512], F32, ps)]
            psO = pt("psO", [128, 2048], F32, ps)
            psT = pt("psT", [128, 1024], BF16, ps)
            kpos = sb("kpos", [128, 512], F32, ps)
            qs = sb("qs", [128, TPC * NKB], F32, ps)

            B.add("sync", lambda s: s.dma_start(out=kpos[:], in_=kpos_in),
                  inc="a_cl", k=16)
            B.add("sync", lambda s: s.dma_start(out=qs[:], in_=qs_in),
                  inc="a_cl", k=16)

            k_lvl = B.val("p1ks")
            v_lvl = B.val("p1vs")
            q_lvl = B.val("qs2")

            kslot_use = [0, 0, 0, 0]
            vslot_use = [0, 0]
            qslot_use = [0, 0]
            for j in range(TPC):
                qsl = j % 2
                B.add("sync", lambda s, j=j, qsl=qsl: s.dma_start(
                    out=qtile[:, qsl, :, :],
                    in_=Q_d.rearrange("a p n -> p a n")[:, :, j * 128:(j + 1) * 128]),
                    deps=[("qs2", q_lvl), ("a_sp", qslot_use[qsl])], inc="a_q", k=16)
                q_ld = B.val("a_q")
                B.add("sync", lambda s, j=j: s.dma_start(
                    out=xot[:], in_=x_own_i[j * 128:(j + 1) * 128, :]),
                    deps=[("a_x1e", B.val("a_x1e"))], inc="a_xo", k=16)
                xo_lvl = B.val("a_xo")

                exp_done_prev = B.val("a_ex")
                vx_prev = B.val("a_vx")
                k_loads = {}
                v_loads = {}
                for p in range(j + 1):
                    for kb in (2 * p, 2 * p + 1):
                        ksl = kb % 4
                        for gg4 in range(4):
                            B.add("sync", lambda s, kb=kb, ksl=ksl, gg4=gg4:
                                  s.dma_start(
                                      out=kring[:, ksl, gg4 * 4:(gg4 + 1) * 4, :],
                                      in_=K_d.rearrange("(a p) n -> p a n", p=128)[
                                          :, gg4 * 4:(gg4 + 1) * 4,
                                          kb * 512:(kb + 1) * 512]),
                                  deps=[("p1ks", k_lvl), ("a_sp", kslot_use[ksl])],
                                  inc="a_k", k=16)
                        k_loads[kb] = B.val("a_k")

                    kb0, kb1 = 2 * p, 2 * p + 1
                    stc_free = B.val("a_stc")
                    for dk in range(NDK):
                        B.add("tensor", lambda tt, qsl=qsl, dk=dk, kb0=kb0: tt.matmul(
                            psS[0][:], qtile[:, qsl, dk, :],
                            kring[:, kb0 % 4, dk, :],
                            start=(dk == 0), stop=(dk == NDK - 1)),
                            deps=[("a_q", q_ld), ("a_k", k_loads[kb0]),
                                  ("a_stc", stc_free)])
                        B.add("tensor", lambda tt, qsl=qsl, dk=dk, kb1=kb1: tt.matmul(
                            psS[1][:], qtile[:, qsl, dk, :],
                            kring[:, kb1 % 4, dk, :],
                            start=(dk == 0), stop=(dk == NDK - 1)),
                            deps=[("a_k", k_loads[kb1])],
                            inc="a_sp" if dk == NDK - 1 else None)
                    sp_lvl = B.val("a_sp")
                    kslot_use[kb0 % 4] = sp_lvl
                    kslot_use[kb1 % 4] = sp_lvl

                    for half, kb in ((0, kb0), (1, kb1)):
                        B.add("vector", lambda v, half=half, j=j, kb=kb: v.tensor_scalar(
                            out=mk[:, half, :], in0=kpos[:],
                            scalar1=qs[:, j * NKB + kb: j * NKB + kb + 1],
                            scalar2=-1e9, op0=ALU.is_gt, op1=ALU.mult),
                            deps=[("a_stc", B.val("a_stc")), ("a_cl", 32),
                                  ("a_mk", B.val("a_mk"))], inc="a_mk")
                        B.add("vector", lambda v, half=half, kb=kb:
                              v.scalar_tensor_tensor(
                                  out=stash[:, kb * 512:(kb + 1) * 512],
                                  in0=psS[half][:], scalar=2048.0 ** -0.5,
                                  in1=mk[:, half, :],
                                  op0=ALU.mult, op1=ALU.add),
                            deps=[("a_sp", sp_lvl), ("a_mk", B.val("a_mk")),
                                  ("a_ex", exp_done_prev), ("a_vx", vx_prev),
                                  ("a_stc", B.val("a_stc"))],
                            inc="a_stc")
                qslot_use[qsl] = B.val("a_sp")

                nkb_j = 2 * j + 2
                B.add("vector", lambda v, nkb_j=nkb_j: v.reduce_max(
                    out=stats[:, 8:9], in_=stash[:, 0:nkb_j * 512], axis=AX.X),
                    deps=[("a_stc", B.val("a_stc")), ("a_vx", B.val("a_vx"))],
                    inc="a_vx")
                B.add("vector", lambda v: v.tensor_scalar_mul(
                    out=stats[:, 9:10], in0=stats[:, 8:9], scalar1=-1.0),
                    deps=[("a_vx", B.val("a_vx"))], inc="a_mx")
                mx_lvl = B.val("a_mx")

                for p in range(j + 1):
                    psl = p % 2
                    for kb in (2 * p, 2 * p + 1):
                        vsl = kb % 2
                        B.add("gpsimd", lambda g, kb=kb, vsl=vsl: g.dma_start(
                            out=vring[:, vsl, :, :],
                            in_=V_d.rearrange("(a p) n -> p a n", p=128)[
                                :, kb * 4:kb * 4 + 4, :]),
                            deps=[("p1vs", v_lvl), ("a_pv", vslot_use[vsl])],
                            inc="a_v", k=16)
                        v_loads[kb] = B.val("a_v")
                    B.add("scalar", lambda s, p=p, psl=psl: s.activation(
                        out=Pring[:, psl * 1024:(psl + 1) * 1024],
                        in_=stash[:, p * 1024:(p + 1) * 1024], func=AF.Exp,
                        bias=stats[:, 9:10], scale=1.0,
                        accum_out=lparts[:, p:p + 1]),
                        deps=[("a_mx", mx_lvl), ("a_vx", vx_prev),
                              ("a_ptt", max(0, B.val("a_ptt") - 1)),
                              ("a_ex", B.val("a_ex"))],
                        inc="a_ex")
                    ex_lvl = B.val("a_ex")
                    for qq in range(8):
                        B.add("tensor", lambda tt, psl=psl, qq=qq: tt.transpose(
                            psT[:, qq * 128:(qq + 1) * 128],
                            Pring[:, psl * 1024 + qq * 128:psl * 1024 + (qq + 1) * 128],
                            ident[:]),
                            deps=[("a_ex", ex_lvl), ("a_pte", B.val("a_pte")),
                                  ("c_v", 1)],
                            inc="a_ptt" if qq == 7 else None)
                    ptt_lvl = B.val("a_ptt")
                    B.add("vector", lambda v, psl=psl: v.tensor_copy(
                        out=PTsb[:, psl * 1024:(psl + 1) * 1024], in_=psT[:]),
                        deps=[("a_ptt", ptt_lvl), ("a_pv", max(0, B.val("a_pv") - 1)),
                              ("a_pte", B.val("a_pte")), ("a_vx", B.val("a_vx")),
                              ("a_stc", B.val("a_stc"))],
                        inc="a_pte")
                    pte_lvl = B.val("a_pte")
                    for kb in (2 * p, 2 * p + 1):
                        half = kb % 2
                        for kc in range(4):
                            for dj in range(4):
                                first = (kb == 0 and kc == 0)
                                last = (p == j and kb == 2 * p + 1 and kc == 3
                                        and dj == 3)
                                is_last_mm = (kb == 2 * p + 1 and kc == 3 and dj == 3)
                                B.add("tensor",
                                      lambda tt, psl=psl, half=half, kc=kc, dj=dj,
                                      first=first, last=last, kb=kb: tt.matmul(
                                          psO[:, dj * 512:(dj + 1) * 512],
                                          PTsb[:, psl * 1024 + half * 512 + kc * 128:
                                               psl * 1024 + half * 512 + (kc + 1) * 128],
                                          vring[:, kb % 2, kc, dj * 512:(dj + 1) * 512],
                                          start=first, stop=last),
                                      deps=[("a_pte", pte_lvl),
                                            ("a_v", v_loads[kb]),
                                            ("a_x1e", B.val("a_x1e"))],
                                      inc="a_pv" if is_last_mm and kb == 2 * p + 1
                                      else None)
                    vslot_use[0] = B.val("a_pv")
                    vslot_use[1] = B.val("a_pv")

                pv_lvl = B.val("a_pv")
                B.add("vector", lambda v, j=j: v.reduce_sum(
                    out=stats[:, 10:11], in_=lparts[:, 0:j + 1], axis=AX.X),
                    deps=[("a_ex", B.val("a_ex")), ("a_vx", B.val("a_vx"))],
                    inc="a_vx")
                B.add("vector", lambda v: v.reciprocal(
                    out=stats[:, 11:12], in_=stats[:, 10:11]),
                    deps=[("a_vx", B.val("a_vx"))], inc="a_vx")
                B.add("vector", lambda v: v.scalar_tensor_tensor(
                    out=x1t[:], in0=psO[:], scalar=stats[:, 11:12], in1=xot[:],
                    op0=ALU.mult, op1=ALU.add),
                    deps=[("a_vx", B.val("a_vx")), ("a_pv", pv_lvl),
                          ("a_xo", xo_lvl), ("a_x1s", 16 * j)],
                    inc="a_x1e")
                B.add("gpsimd", lambda g, j=j: g.dma_start(
                    out=X1_d[j * 128:(j + 1) * 128, :], in_=x1t[:]),
                    deps=[("a_x1e", B.val("a_x1e"))], inc="a_x1s", k=16)
            B.emit(block)
        B.barrier("b3")
        B.emit(block)

        with contextlib.ExitStack() as ps:
            wt = sb("wt", [128, 4, 2048], F32, ps)
            acc3 = sb("acc3", [128, 4], F32, ps)
            tcol = sb("tcol", [128, 4], F32, ps)
            gscr = sb("gscr", [128, 2048], F32, ps)
            psS1 = pt("psS1", [1, 16], F32, ps)
            psB = pt("psB", [128, 16], F32, ps)

            B.add("vector", lambda v: v.memset(acc3[:], 0.0),
                  deps=[("g_v", 0)], inc="g_v")
            mats = [(gw_i, 0), (vw_i, 1), (ow_i, 2)]
            for m_ap, mi in mats:
                for i in range(64):
                    sl = i % 4
                    if mi == 2:
                        r, cc = i % 16, i // 16
                        src = m_ap[r * 128:(r + 1) * 128, cc * 2048:(cc + 1) * 2048]
                    else:
                        src = m_ap[i * 128:(i + 1) * 128, :]
                    B.add("sync", lambda s, src=src, sl=sl: s.dma_start(
                        out=wt[:, sl, :], in_=src),
                        deps=[("g_u", max(0, B.val("g_u") - 3))], inc="g_l", k=16)
                    ld = B.val("g_l")
                    if i % 2 == 1:
                        B.add("scalar", lambda s, sl=sl: s.activation(
                            out=gscr[:], in_=wt[:, sl, :], func=AF.Abs,
                            accum_out=tcol[:, 1:2]),
                            deps=[("g_l", ld), ("g_u", B.val("g_u")),
                                  ("g_s", B.val("g_s"))], inc="g_s")
                        B.add("vector", lambda v, mi=mi: v.tensor_tensor(
                            out=acc3[:, mi:mi + 1], in0=acc3[:, mi:mi + 1],
                            in1=tcol[:, 1:2], op=ALU.add),
                            deps=[("g_s", B.val("g_s")), ("g_v", B.val("g_v")),
                                  ("g_u", B.val("g_u"))],
                            inc="g_u")
                    else:
                        B.add("vector", lambda v, sl=sl: v.reduce_sum(
                            out=tcol[:, 0:1], in_=wt[:, sl, :], axis=AX.X,
                            apply_absolute_value=True),
                            deps=[("g_l", ld), ("g_v", B.val("g_v")),
                                  ("g_u", B.val("g_u"))], inc="g_v")
                        B.add("vector", lambda v, mi=mi: v.tensor_tensor(
                            out=acc3[:, mi:mi + 1], in0=acc3[:, mi:mi + 1],
                            in1=tcol[:, 0:1], op=ALU.add),
                            deps=[("g_v", B.val("g_v"))], inc="g_u")
            full_u = B.val("g_u")
            for mi in range(3):
                B.add("tensor", lambda tt, mi=mi: tt.matmul(
                    psS1[0:1, mi:mi + 1], acc3[:, mi:mi + 1], ones_cf[:],
                    start=True, stop=True),
                    deps=[("g_u", full_u), ("c_g", cg)], inc="g_t")
            t_lvl = B.val("g_t")
            B.add("vector", lambda v: v.tensor_scalar_mul(
                out=gamw_s[0:1, 0:3], in0=psS1[0:1, 0:3], scalar1=1.0 / (HID * D)),
                deps=[("g_t", t_lvl), ("g_v", B.val("g_v"))], inc="g_v")
            B.add("vector", lambda v: v.tensor_scalar_max(
                out=gamw_s[0:1, 0:3], in0=gamw_s[0:1, 0:3], scalar1=EPS),
                deps=[("g_v", B.val("g_v"))], inc="g_v2")
            sc_lvl = B.val("g_v2")
            for mi in range(3):
                B.add("tensor", lambda tt, mi=mi: tt.matmul(
                    psB[:, mi:mi + 1], ones_rf[0:1, :], gamw_s[0:1, mi:mi + 1],
                    start=True, stop=True),
                    deps=[("g_v2", sc_lvl), ("c_g", cg)], inc="g_bc")
            bc_lvl = B.val("g_bc")
            B.add("vector", lambda v: v.tensor_scalar_mul(
                out=gamw[:, 0:3], in0=psB[:, 0:3], scalar1=1.0 / 127.0),
                deps=[("g_bc", bc_lvl), ("g_v2", B.val("g_v2"))], inc="g_f")
            B.add("vector", lambda v: v.tensor_scalar_mul(
                out=thrp[:, 0:3], in0=gamw[:, 0:3], scalar1=63.5),
                deps=[("g_f", B.val("g_f"))], inc="g_f")
            B.add("vector", lambda v: v.tensor_scalar_mul(
                out=thrn[:, 0:3], in0=gamw[:, 0:3], scalar1=-63.5),
                deps=[("g_f", B.val("g_f"))], inc="g_f")
            B.emit(block)
        B.barrier("b4")
        B.emit(block)

        with contextlib.ExitStack() as ps:
            wt = sb("wt4", [128, 4, 2048], F32, ps)
            cn = sb("cn4", [128, 2048], F32, ps)
            wqs = sb("wqs", [128, 2, 2048], BF16, ps)

            caches = [(gw_i, GWC, 0), (vw_i, VWC, 1), (ow_i, OWC, 2)]
            wq_store = [0, 0]
            for m_ap, c_ap, mi in caches:
                for i in range(64):
                    sl = i % 4
                    osl = i % 2
                    if mi == 2:
                        r, cc = i % 16, i // 16
                        src = m_ap[r * 128:(r + 1) * 128, cc * 2048:(cc + 1) * 2048]
                        dst = c_ap[r * 128:(r + 1) * 128, cc * 2048:(cc + 1) * 2048]
                    else:
                        src = m_ap[i * 128:(i + 1) * 128, :]
                        dst = c_ap[i * 128:(i + 1) * 128, :]
                    B.add("sync", lambda s, src=src, sl=sl: s.dma_start(
                        out=wt[:, sl, :], in_=src),
                        deps=[("q4u", max(0, B.val("q4u") - 3))], inc="q4l", k=16)
                    ld = B.val("q4l")
                    B.add("vector", lambda v, sl=sl, mi=mi: v.tensor_scalar(
                        out=cn[:], in0=wt[:, sl, :], scalar1=thrn[:, mi:mi + 1],
                        scalar2=None, op0=ALU.is_le),
                        deps=[("q4l", ld), ("q4u", B.val("q4u")),
                              ("g_f", 3)], inc="q4v")
                    B.add("vector", lambda v, sl=sl, osl=osl, mi=mi:
                          v.scalar_tensor_tensor(
                              out=wqs[:, osl, :], in0=wt[:, sl, :],
                              scalar=thrp[:, mi:mi + 1], in1=cn[:],
                              op0=ALU.is_ge, op1=ALU.subtract),
                          deps=[("q4v", B.val("q4v")), ("q4s", wq_store[osl])],
                          inc="q4u")
                    B.add("gpsimd", lambda g, dst=dst, osl=osl: g.dma_start(
                        out=dst, in_=wqs[:, osl, :]),
                        deps=[("q4u", B.val("q4u"))], inc="q4s", k=16)
                    wq_store[osl] = B.val("q4s")
            B.emit(block)
        B.barrier("b5")
        B.emit(block)

        p45 = contextlib.ExitStack()
        xq2_fm = sb("xq2_fm", [128, NDK, 1024], BF16, p45)
        sinv_g = sb("sinv_g", [1, 128 * TPC], BF16, p45)
        sinv_v = sb("sinv_v", [1, 128 * TPC], BF16, p45)
        with contextlib.ExitStack() as ps:
            bx = sb("bx", [128, 2 * D], F32, ps)
            bxc = sb("bxc", [128, D], F32, ps)
            bscr = sb("bscr", [128, D], F32, ps)
            bh2 = sb("bh2", [128, D], F32, ps)
            bti = sb("bti", [128, D], I32, ps)
            bfa = sb("bfa", [128, D], F32, ps)
            bxq = sb("bxq", [128, 2, D], BF16, ps)
            brow = sb("brow", [8, 128], F32, ps)
            psR = pt("psR", [8, 128], F32, ps)

            x1_lvl = B.val("a_x1s")
            xq_store = [0, 0]
            for t in range(TPC):
                slx = t % 2
                B.add("sync", lambda s, t=t, slx=slx: s.dma_start(
                    out=bx[:, slx * D:(slx + 1) * D],
                    in_=X1_d[t * 128:(t + 1) * 128, :]),
                    deps=[("a_x1s", x1_lvl), ("bxq_s", 16 * (t - 1))],
                    inc="bxl", k=16)
                plan_ln(bx[:, slx * D:(slx + 1) * D], bh2[:], "bv", "bs",
                        bxc[:], bscr[:],
                        extra_dep=[("bxl", 16 * (t + 1)), ("bxq_s", 16 * (t - 1))])
                B.add("vector", lambda v: v.reduce_max(
                    out=stats[:, 5:6], in_=bh2[:], axis=AX.X,
                    apply_absolute_value=True),
                    deps=[("bv", B.val("bv"))], inc="bv")
                B.add("vector", lambda v, t=t: v.tensor_scalar_max(
                    out=gamx[:, t:t + 1], in0=stats[:, 5:6], scalar1=EPS),
                    deps=[("bv", B.val("bv"))], inc="bv")
                B.add("vector", lambda v, t=t: v.reciprocal(
                    out=stats[:, 6:7], in_=gamx[:, t:t + 1]),
                    deps=[("bv", B.val("bv"))], inc="bv")
                B.add("vector", lambda v: v.tensor_scalar_mul(
                    out=stats[:, 6:7], in0=stats[:, 6:7], scalar1=127.0),
                    deps=[("bv", B.val("bv"))], inc="bv")
                B.add("vector", lambda v: v.tensor_scalar(
                    out=bxc[:], in0=bh2[:], scalar1=stats[:, 6:7], scalar2=None,
                    op0=ALU.mult),
                    deps=[("bv", B.val("bv")), ("bs", B.val("bs"))], inc="bv")
                plan_round(bxc[:], bxq[:, slx, :], bti[:], bfa[:], bscr[:], "bv")
                B.add("gpsimd", lambda g, t=t, slx=slx: g.dma_start(
                    out=XQ_d[t * 128:(t + 1) * 128, :], in_=bxq[:, slx, :]),
                    deps=[("bv", B.val("bv"))], inc="bxq_s", k=16)
                B.add("vector", lambda v, t=t: v.tensor_tensor(
                    out=sg_all[:, t:t + 1], in0=gamx[:, t:t + 1],
                    in1=gamw[:, 0:1], op=ALU.mult),
                    deps=[("bv", B.val("bv")), ("g_f", 3)], inc="bv")
                B.add("vector", lambda v, t=t: v.tensor_tensor(
                    out=sv_all[:, t:t + 1], in0=gamx[:, t:t + 1],
                    in1=gamw[:, 1:2], op=ALU.mult),
                    deps=[("bv", B.val("bv"))], inc="bv")
                B.add("vector", lambda v, t=t: v.reciprocal(
                    out=sinvg_c[:, t:t + 1], in_=sg_all[:, t:t + 1]),
                    deps=[("bv", B.val("bv"))], inc="bv")
                B.add("vector", lambda v, t=t: v.reciprocal(
                    out=sinvv_c[:, t:t + 1], in_=sv_all[:, t:t + 1]),
                    deps=[("bv", B.val("bv"))], inc="bv")

            for ii, col in ((0, sinvg_c), (1, sinvv_c)):
                B.add("tensor", lambda tt, col=col: tt.transpose(
                    psR[:, :], col[:], ident32[:]),
                    deps=[("bv", B.val("bv")), ("c_v", 2),
                          ("brow_e", B.val("brow_e"))], inc="brow_t")
                B.add("vector", lambda v: v.tensor_copy(out=brow[:], in_=psR[:]),
                      deps=[("brow_t", B.val("brow_t")), ("bv", B.val("bv")),
                            ("brow_s", B.val("brow_s"))],
                      inc="brow_e")
                B.add("gpsimd", lambda g, ii=ii: g.dma_start(
                    out=SB_d[ii], in_=brow[:]),
                    deps=[("brow_e", B.val("brow_e"))], inc="brow_s", k=16)
            B.add("gpsimd", lambda g: g.dma_start(
                out=sinv_g[:], in_=SB_d[0:1].rearrange("a b c -> a (b c)")),
                deps=[("brow_s", 16)], inc="sinv_l", k=16)
            B.add("gpsimd", lambda g: g.dma_start(
                out=sinv_v[:], in_=SB_d[1:2].rearrange("a b c -> a (b c)")),
                deps=[("brow_s", 32)], inc="sinv_l", k=16)

            xq_lvl = B.val("bxq_s")
            for ck in range(NDK):
                B.add("sync", lambda s, ck=ck: s.dma_start(
                    out=xq2_fm[:, ck, :],
                    in_=XQ_d[0:1024, ck * 128:(ck + 1) * 128], transpose=True),
                    deps=[("bxq_s", xq_lvl)], inc="xq2t", k=16)
            B.emit(block)
        B.barrier("b6")
        B.emit(block)

        with contextlib.ExitStack() as ps:
            wgT = sb("wgT", [128, 2, NDK, 512], BF16, ps)
            wvT = sb("wvT", [128, 2, NDK, 512], BF16, ps)
            gsb = sb("gsb", [128, 2, 512], F32, ps)
            vsb = sb("vsb", [128, 2, 512], F32, ps)
            gvsb = sb("gvsb", [128, 2, 512], F32, ps)
            pg = [pt("pg0", [128, 512], F32, ps), pt("pg1", [128, 512], F32, ps)]
            pv = [pt("pv0", [128, 512], F32, ps), pt("pv1", [128, 512], F32, ps)]
            gb_row = sb("gb_row", [1, HID], BF16, ps)
            vb_row = sb("vb_row", [1, HID], BF16, ps)

            B.add("gpsimd", lambda g: g.dma_start(out=gb_row[:], in_=gb_i),
                  inc="p5bl", k=16)
            B.add("gpsimd", lambda g: g.dma_start(out=vb_row[:], in_=vb_i),
                  inc="p5bl", k=16)
            gwc_lvl = B.val("q4s")
            xq2_lvl = B.val("xq2t")
            sinv_lvl = B.val("sinv_l")
            wring_use = [0, 0]
            pslot_free = [0, 0]
            gv_store = [0, 0]
            for hw in range(16):
                wsl = hw % 2
                for ck in range(NDK):
                    B.add("sync", lambda s, hw=hw, wsl=wsl, ck=ck: s.dma_start(
                        out=wgT[:, wsl, ck, :],
                        in_=GWC[hw * 512:(hw + 1) * 512, ck * 128:(ck + 1) * 128],
                        transpose=True),
                        deps=[("q4s", gwc_lvl), ("p5mm", wring_use[wsl])],
                        inc="p5w", k=16)
                    B.add("sync", lambda s, hw=hw, wsl=wsl, ck=ck: s.dma_start(
                        out=wvT[:, wsl, ck, :],
                        in_=VWC[hw * 512:(hw + 1) * 512, ck * 128:(ck + 1) * 128],
                        transpose=True),
                        deps=[], inc="p5w", k=16)
                w_lvl = B.val("p5w")
                for t_i in range(TPC):
                    sl = t_i % 2
                    for ck in range(NDK):
                        B.add("tensor", lambda tt, sl=sl, wsl=wsl, ck=ck, t_i=t_i:
                              tt.matmul(
                                  pg[sl][:], xq2_fm[:, ck, t_i * 128:(t_i + 1) * 128],
                                  wgT[:, wsl, ck, :],
                                  start=(ck == 0), stop=False),
                              deps=[("p5w", w_lvl), ("xq2t", xq2_lvl),
                                    ("p5gv", pslot_free[sl])])
                        B.add("tensor", lambda tt, sl=sl, wsl=wsl, ck=ck, t_i=t_i:
                              tt.matmul(
                                  pv[sl][:], xq2_fm[:, ck, t_i * 128:(t_i + 1) * 128],
                                  wvT[:, wsl, ck, :],
                                  start=(ck == 0), stop=False),
                              deps=[])
                    B.add("tensor", lambda tt, sl=sl, hw=hw, t_i=t_i: tt.matmul(
                        pg[sl][:], sinv_g[0:1, t_i * 128:(t_i + 1) * 128],
                        gb_row[0:1, hw * 512:(hw + 1) * 512],
                        start=False, stop=True),
                        deps=[("sinv_l", sinv_lvl), ("p5bl", 32)])
                    B.add("tensor", lambda tt, sl=sl, hw=hw, t_i=t_i: tt.matmul(
                        pv[sl][:], sinv_v[0:1, t_i * 128:(t_i + 1) * 128],
                        vb_row[0:1, hw * 512:(hw + 1) * 512],
                        start=False, stop=True),
                        deps=[], inc="p5mm")
                    mm_lvl = B.val("p5mm")
                    B.add("scalar", lambda s, sl=sl, t_i=t_i: s.activation(
                        out=gsb[:, sl, :], in_=pg[sl][:], func=AF.Silu,
                        scale=sg_all[:, t_i:t_i + 1]),
                        deps=[("p5mm", mm_lvl), ("p5gv", B.val("p5gv")),
                              ("p5s", B.val("p5s"))],
                        inc="p5s")
                    B.add("vector", lambda v, sl=sl, t_i=t_i: v.tensor_scalar(
                        out=vsb[:, sl, :], in0=pv[sl][:],
                        scalar1=sv_all[:, t_i:t_i + 1], scalar2=None, op0=ALU.mult),
                        deps=[("p5mm", mm_lvl), ("p5gv", B.val("p5gv")),
                              ("p5v", B.val("p5v"))],
                        inc="p5v")
                    B.add("vector", lambda v, sl=sl: v.tensor_tensor(
                        out=gvsb[:, sl, :], in0=gsb[:, sl, :], in1=vsb[:, sl, :],
                        op=ALU.mult),
                        deps=[("p5s", B.val("p5s")), ("p5v", B.val("p5v")),
                              ("p5gs", gv_store[sl])],
                        inc="p5gv")
                    pslot_free[sl] = B.val("p5gv")
                    B.add("vector", lambda v, sl=sl, t_i=t_i, hw=hw: v.reduce_max(
                        out=gvm_all[:, t_i * 16 + hw:t_i * 16 + hw + 1],
                        in_=gvsb[:, sl, :], axis=AX.X, apply_absolute_value=True),
                        deps=[("p5gv", B.val("p5gv"))], inc="p5v")
                    B.add("gpsimd", lambda g, sl=sl, t_i=t_i, hw=hw: g.dma_start(
                        out=GV_d[t_i * 128:(t_i + 1) * 128,
                                 hw * 512:(hw + 1) * 512],
                        in_=gvsb[:, sl, :]),
                        deps=[("p5v", B.val("p5v"))], inc="p5gs", k=16)
                    gv_store[sl] = B.val("p5gs")
                wring_use[wsl] = B.val("p5mm")
            B.emit(block)
        B.barrier("b7")
        B.emit(block)
        p45.close()

        with contextlib.ExitStack() as ps:
            gvt = sb("gvt", [128, 4, 512], F32, ps)
            gti = sb("gti", [128, 512], I32, ps)
            gfa = sb("gfa", [128, 512], F32, ps)
            gfb = sb("gfb", [128, 512], F32, ps)
            gvq = sb("gvq", [128, 2, 512], BF16, ps)
            gvqT = sb("gvqT", [128, 64, 1024], BF16, ps)
            owc_sb = sb("owc_sb", [128, 4, 256], BF16, ps)
            yfm = sb("yfm", [128, 512], F32, ps)
            y3st = sb("y3st", [128, 2, 128], F32, ps)
            po = [pt("po%d" % i, [128, 512], F32, ps) for i in range(4)]
            ptp = pt("ptp", [128, 512], F32, ps)
            outb_sb = sb("outb_sb", [128, D], F32, ps)

            B.add("sync", lambda s: s.dma_start(out=outb_sb[:], in_=outb_i),
                  inc="p6bl", k=16)
            gv_lvl = B.val("p5gs")
            for t_i in range(TPC):
                B.add("vector", lambda v, t_i=t_i: v.reduce_max(
                    out=stats[:, 5:6], in_=gvm_all[:, t_i * 16:(t_i + 1) * 16],
                    axis=AX.X),
                    deps=[("p5v", B.val("p5v")), ("c6v", B.val("c6v"))], inc="c6v")
                B.add("vector", lambda v, t_i=t_i: v.tensor_scalar_max(
                    out=stats[:, 5:6], in0=stats[:, 5:6], scalar1=EPS),
                    deps=[("c6v", B.val("c6v"))], inc="c6v")
                B.add("vector", lambda v, t_i=t_i: v.tensor_tensor(
                    out=so_all[:, t_i:t_i + 1], in0=stats[:, 5:6],
                    in1=gamw[:, 2:3], op=ALU.mult),
                    deps=[("c6v", B.val("c6v")), ("g_f", 3)], inc="c6v")
                B.add("vector", lambda v, t_i=t_i: v.reciprocal(
                    out=stats[:, 6:7], in_=stats[:, 5:6]),
                    deps=[("c6v", B.val("c6v"))], inc="c6v")
                B.add("vector", lambda v, t_i=t_i: v.tensor_scalar_mul(
                    out=gamx[:, t_i:t_i + 1], in0=stats[:, 6:7], scalar1=127.0),
                    deps=[("c6v", B.val("c6v"))], inc="c6v")
            sc6 = B.val("c6v")

            gvq_store = [0, 0]
            for t_i in range(TPC):
                for hw in range(16):
                    sl = (t_i * 16 + hw) % 4
                    osl = (t_i * 16 + hw) % 2
                    B.add("sync", lambda s, t_i=t_i, hw=hw, sl=sl: s.dma_start(
                        out=gvt[:, sl, :],
                        in_=GV_d[t_i * 128:(t_i + 1) * 128,
                                 hw * 512:(hw + 1) * 512]),
                        deps=[("p5gs", gv_lvl), ("c6u", max(0, B.val("c6u") - 3))],
                        inc="c6l", k=16)
                    ld = B.val("c6l")
                    B.add("vector", lambda v, sl=sl, t_i=t_i: v.tensor_scalar(
                        out=gfa[:], in0=gvt[:, sl, :],
                        scalar1=gamx[:, t_i:t_i + 1], scalar2=None, op0=ALU.mult),
                        deps=[("c6l", ld), ("c6v", B.val("c6v")), ("c6u", B.val("c6u"))],
                        inc="c6v")
                    B.add("vector", lambda v: v.tensor_copy(out=gti[:], in_=gfa[:]),
                          deps=[("c6v", B.val("c6v"))], inc="c6v")
                    B.add("vector", lambda v: v.tensor_sub(
                        out=gfb[:], in0=gfa[:], in1=gti[:]),
                        deps=[("c6v", B.val("c6v"))], inc="c6v")
                    B.add("vector", lambda v: v.tensor_single_scalar(
                        out=gfa[:], in_=gfb[:], scalar=-0.5, op=ALU.is_lt),
                        deps=[("c6v", B.val("c6v"))], inc="c6v")
                    B.add("vector", lambda v: v.scalar_tensor_tensor(
                        out=gfa[:], in0=gfb[:], scalar=0.5, in1=gfa[:],
                        op0=ALU.is_gt, op1=ALU.subtract),
                        deps=[("c6v", B.val("c6v"))], inc="c6v")
                    B.add("vector", lambda v, osl=osl: v.tensor_add(
                        out=gvq[:, osl, :], in0=gfa[:], in1=gti[:]),
                        deps=[("c6v", B.val("c6v")), ("c6s", gvq_store[osl])],
                        inc="c6u")
                    B.add("gpsimd", lambda g, t_i=t_i, hw=hw, osl=osl: g.dma_start(
                        out=GVQ_d[t_i * 128:(t_i + 1) * 128,
                                  hw * 512:(hw + 1) * 512],
                        in_=gvq[:, osl, :]),
                        deps=[("c6u", B.val("c6u"))], inc="c6s", k=16)
                    gvq_store[osl] = B.val("c6s")
            gvq_lvl = B.val("c6s")

            for hk in range(64):
                B.add("sync", lambda s, hk=hk: s.dma_start(
                    out=gvqT[:, hk, :],
                    in_=GVQ_d[0:1024, hk * 128:(hk + 1) * 128], transpose=True),
                    deps=[("c6s", gvq_lvl)], inc="c6t", k=16)
            gvqT_lvl = B.val("c6t")

            ow_lvl = B.val("q4s")
            ow_use = [0, 0, 0, 0]
            y3_store = [0, 0]
            for dg in range(8):
                for hk in range(64):
                    sl = hk % 4
                    B.add("sync", lambda s, dg=dg, hk=hk, sl=sl: s.dma_start(
                        out=owc_sb[:, sl, :],
                        in_=OWC[dg * 256:(dg + 1) * 256, hk * 128:(hk + 1) * 128],
                        transpose=True),
                        deps=[("q4s", ow_lvl), ("p6mm", ow_use[sl])],
                        inc="p6w", k=16)
                    w_lvl = B.val("p6w")
                    for d_i in range(2):
                        for su in range(2):
                            pidx = d_i * 2 + su
                            B.add("tensor",
                                  lambda tt, sl=sl, d_i=d_i, su=su, pidx=pidx, hk=hk:
                                  tt.matmul(
                                      po[pidx][:],
                                      owc_sb[:, sl, d_i * 128:(d_i + 1) * 128],
                                      gvqT[:, hk, su * 512:(su + 1) * 512],
                                      start=(hk == 0), stop=(hk == 63)),
                                  deps=[("p6w", w_lvl), ("c6t", gvqT_lvl),
                                        ("p6e", B.val("p6e") if hk == 0 else 0)],
                                  inc="p6mm" if pidx == 3 else None)
                    ow_use[sl] = B.val("p6mm")
                mm_lvl = B.val("p6mm")
                for d_i in range(2):
                    d_gl = dg * 2 + d_i
                    for su in range(2):
                        pidx = d_i * 2 + su
                        B.add("vector", lambda v, pidx=pidx: v.tensor_copy(
                            out=yfm[:], in_=po[pidx][:]),
                            deps=[("p6mm", mm_lvl), ("c6v", B.val("c6v")),
                                  ("p6tt", B.val("p6tt")), ("p6e", B.val("p6e"))],
                            inc="p6e")
                        ev_lvl = B.val("p6e")
                        for qq in range(4):
                            B.add("tensor", lambda tt, qq=qq: tt.transpose(
                                ptp[:, qq * 128:(qq + 1) * 128],
                                yfm[:, qq * 128:(qq + 1) * 128], ident32[:]),
                                deps=[("p6e", ev_lvl), ("c_v", 2),
                                      ("p6y", B.val("p6y"))],
                                inc="p6tt" if qq == 3 else None)
                        tt_lvl = B.val("p6tt")
                        for qq in range(4):
                            tt_tile = su * 4 + qq
                            ysl = qq % 2
                            B.add("vector",
                                  lambda v, qq=qq, tt_tile=tt_tile, d_gl=d_gl, ysl=ysl:
                                  v.scalar_tensor_tensor(
                                      out=y3st[:, ysl, :],
                                      in0=ptp[:, qq * 128:(qq + 1) * 128],
                                      scalar=so_all[:, tt_tile:tt_tile + 1],
                                      in1=outb_sb[:, d_gl * 128:(d_gl + 1) * 128],
                                      op0=ALU.mult, op1=ALU.add),
                                  deps=[("p6tt", tt_lvl), ("c6v", sc6),
                                        ("p6ys", y3_store[ysl]), ("p6bl", 16),
                                        ("p6y", B.val("p6y"))],
                                  inc="p6y")
                            B.add("gpsimd",
                                  lambda g, tt_tile=tt_tile, d_gl=d_gl, ysl=ysl:
                                  g.dma_start(
                                      out=Y3_d[tt_tile * 128:(tt_tile + 1) * 128,
                                               d_gl * 128:(d_gl + 1) * 128],
                                      in_=y3st[:, ysl, :]),
                                  deps=[("p6y", B.val("p6y"))], inc="p6ys", k=16)
                            y3_store[ysl] = B.val("p6ys")
            B.emit(block)
        B.barrier("b8")
        B.emit(block)

        with contextlib.ExitStack() as ps:
            f1 = sb("f1", [128, 2 * D], F32, ps)
            f2 = sb("f2", [128, 2 * D], F32, ps)
            fo = sb("fo", [128, 2, D], F32, ps)
            y3_lvl = B.val("p6ys")
            x1_lvl = B.val("a_x1s")
            for t in range(TPC):
                slx = t % 2
                B.add("sync", lambda s, t=t, slx=slx: s.dma_start(
                    out=f1[:, slx * D:(slx + 1) * D],
                    in_=X1_d[t * 128:(t + 1) * 128, :]),
                    deps=[("a_x1s", x1_lvl), ("fys", 16 * (t - 1))], inc="fl", k=16)
                B.add("sync", lambda s, t=t, slx=slx: s.dma_start(
                    out=f2[:, slx * D:(slx + 1) * D],
                    in_=Y3_d[t * 128:(t + 1) * 128, :]),
                    deps=[("p6ys", y3_lvl)], inc="fl", k=16)
                ld = B.val("fl")
                B.add("vector", lambda v, slx=slx: v.tensor_add(
                    out=fo[:, slx, :], in0=f1[:, slx * D:(slx + 1) * D],
                    in1=f2[:, slx * D:(slx + 1) * D]),
                    deps=[("fl", ld), ("fys", 16 * (t - 1)), ("fv", B.val("fv"))],
                    inc="fv")
                B.add("gpsimd", lambda g, t=t, slx=slx: g.dma_start(
                    out=y_out[t * 128:(t + 1) * 128, :], in_=fo[:, slx, :]),
                    deps=[("fv", B.val("fv"))], inc="fys", k=16)
            B.add("gpsimd", None, deps=[("fys", B.val("fys"))])
            if debug:
                B.add("gpsimd", lambda g: g.dma_start(out=dbg["dbg_h"], in_=H_d),
                      deps=[("p1hs", B.val("p1hs"))], inc="dbs", k=16)
                B.add("gpsimd", lambda g: g.dma_start(out=dbg["dbg_k"], in_=K_d),
                      deps=[("p1ks", B.val("p1ks"))], inc="dbs", k=16)
                B.add("gpsimd", lambda g: g.dma_start(out=dbg["dbg_v"], in_=V_d),
                      deps=[("p1vs", B.val("p1vs"))], inc="dbs", k=16)
                B.add("gpsimd", lambda g: g.dma_start(out=dbg["dbg_x1"], in_=X1_d),
                      deps=[("a_x1s", B.val("a_x1s"))], inc="dbs", k=16)
                B.add("gpsimd", lambda g: g.dma_start(
                    out=dbg["dbg_gam"], in_=gamw_s[0:1, :]),
                    deps=[("g_f", B.val("g_f"))], inc="dbs", k=16)
                B.add("gpsimd", lambda g: g.dma_start(out=dbg["dbg_gv"], in_=GV_d),
                      deps=[("p5gs", B.val("p5gs"))], inc="dbs", k=16)
                B.add("gpsimd", lambda g: g.dma_start(out=dbg["dbg_xq"], in_=XQ_d),
                      deps=[("bxq_s", B.val("bxq_s"))], inc="dbs", k=16)
                B.add("gpsimd", lambda g: g.dma_start(out=dbg["dbg_gwq"], in_=GWC),
                      deps=[("q4s", B.val("q4s"))], inc="dbs", k=16)
                B.add("gpsimd", lambda g: g.dma_start(out=dbg["dbg_gvq"], in_=GVQ_d),
                      deps=[("c6s", B.val("c6s"))], inc="dbs", k=16)
                B.add("gpsimd", lambda g: g.dma_start(out=dbg["dbg_y3"], in_=Y3_d),
                      deps=[("p6ys", B.val("p6ys"))], inc="dbs", k=16)
                B.add("gpsimd", None, deps=[("dbs", B.val("dbs"))])
            B.emit(block)

    return nc


_NC_CACHE = {}


def _get_nc(debug=DEBUG):
    key = bool(debug)
    if key not in _NC_CACHE:
        _NC_CACHE[key] = build_nc(debug)
    return _NC_CACHE[key]


def kernel(L0_attn_x, L0_attn_Wq, L0_attn_bq, L0_attn_Wk, L0_attn_bk,
           L0_attn_Wv, L0_attn_bv, L0_ffn_gate_w, L0_ffn_gate_b,
           L0_ffn_val_w, L0_ffn_val_b, L0_ffn_out_w, L0_ffn_out_b,
           _debug=False):
    x = np.ascontiguousarray(np.asarray(L0_attn_x, dtype=np.float32).reshape(T, D))
    Wq = np.ascontiguousarray(np.asarray(L0_attn_Wq, dtype=np.float32))
    Wk = np.ascontiguousarray(np.asarray(L0_attn_Wk, dtype=np.float32))
    Wv = np.ascontiguousarray(np.asarray(L0_attn_Wv, dtype=np.float32))
    bq = np.asarray(L0_attn_bq, dtype=np.float32)
    bk = np.asarray(L0_attn_bk, dtype=np.float32)
    bv = np.asarray(L0_attn_bv, dtype=np.float32)
    gw = np.ascontiguousarray(np.asarray(L0_ffn_gate_w, dtype=np.float32))
    vw = np.ascontiguousarray(np.asarray(L0_ffn_val_w, dtype=np.float32))
    ow = np.ascontiguousarray(np.asarray(L0_ffn_out_w, dtype=np.float32))
    gb = np.asarray(L0_ffn_gate_b, dtype=np.float32)
    vb = np.asarray(L0_ffn_val_b, dtype=np.float32)
    ob = np.asarray(L0_ffn_out_b, dtype=np.float32)

    kpos = np.broadcast_to(np.arange(512, dtype=np.float32)[None, :],
                           (128, 512)).copy()
    bq_t = np.ascontiguousarray(bq.reshape(NDK, 128).T)
    bk_t = np.ascontiguousarray(bk.reshape(NDK, 128).T)
    shared = {
        "x_full": x, "kpos_in": kpos,
        "Wq": Wq, "Wk": Wk, "Wv": Wv,
        "bq_t": bq_t, "bk_t": bk_t, "bv_r": bv.reshape(1, D),
        "gate_w": gw, "val_w": vw, "out_w": ow,
        "gate_b_r": gb.reshape(1, HID), "val_b_r": vb.reshape(1, HID),
        "out_b_rep": np.broadcast_to(ob[None, :], (128, D)).copy(),
    }

    in_maps = []
    p_idx = np.arange(128, dtype=np.float32)[:, None]
    for c in range(NCORE):
        tiles = [c + 8 * jj for jj in range(TPC)]
        x_own = np.concatenate([x[g * 128:(g + 1) * 128] for g in tiles], axis=0)
        qsv = np.empty((128, TPC * NKB), dtype=np.float32)
        for jj, g in enumerate(tiles):
            for kb in range(NKB):
                qsv[:, jj * NKB + kb] = g * 128 + p_idx[:, 0] - 512.0 * kb
        m = dict(shared)
        m["x_own"] = np.ascontiguousarray(x_own)
        m["qs_shift"] = qsv
        in_maps.append(m)

    nc = _get_nc(_debug)
    res = run_bass_kernel_spmd(nc, in_maps, core_ids=list(range(NCORE)))

    y = np.empty((T, D), dtype=np.float32)
    for c in range(NCORE):
        yc = np.asarray(res.results[c]["y_out"], dtype=np.float32)
        for jj in range(TPC):
            g = c + 8 * jj
            y[g * 128:(g + 1) * 128] = yc[jj * 128:(jj + 1) * 128]
    if _debug:
        return y.reshape(1, T, D), res
    return y.reshape(1, T, D)
